# version 09
# baseline (speedup 1.0000x reference)
import numpy as np
from contextlib import ExitStack

import concourse.bass as bass
import concourse.mybir as mybir
from concourse.bass_utils import run_bass_kernel_spmd

F32 = mybir.dt.float32
BF16 = mybir.dt.bfloat16
AF = mybir.ActivationFunctionType
ALU = mybir.AluOpType
AX = mybir.AxisListType

D = 1024
KCN = 8
H = 16
DH = 64
M = 266
MCS = (128, 128, 10)
HID = 4096
NHC = 32
TT = 512
G = 4
N = 4096
LN_EPS = 1e-5
EPS = 1e-4
DN = float(DH) ** -0.25

ENGINES = ("pe", "dve", "act", "pool", "sp")
NDMASEM = 8


class Em:

    def __init__(self):
        self.cnt = {}
        self.snap = {}
        self.vc = {e: {} for e in ENGINES}
        self.lastw = {}
        self.readers = {}
        self.q = {e: [] for e in ENGINES}
        self.dmarot = {e: 0 for e in ENGINES}

    def emit(self, eng, fn, reads=(), writes=(), dma=False, acc=False):
        q = self.q[eng]
        deps = {}

        def add(evs):
            if not evs:
                return
            for s, c in evs.items():
                if deps.get(s, 0) < c:
                    deps[s] = c
        for r in reads:
            add(self.lastw.get(r))
        for w in writes:
            if not acc:
                add(self.lastw.get(w))
            add(self.readers.get(w))
        if dma:
            i = self.dmarot[eng]
            self.dmarot[eng] = (i + 1) % NDMASEM
            s_done = f"d:{eng}{i}"
            delta = 16
            prev = self.cnt.get(s_done, 0)
            if prev and deps.get(s_done, 0) < prev:
                deps[s_done] = prev
        else:
            s_done, delta = "e:" + eng, 1
        myvc = self.vc[eng]
        for s, c in sorted(deps.items()):
            if myvc.get(s, 0) >= c:
                continue
            if eng == "pe" and s == "e:pe":
                continue
            q.append(("w", s, c))
            sn = self.snap.get((s, c))
            if sn:
                for k, v in sn.items():
                    if myvc.get(k, 0) < v:
                        myvc[k] = v
            if myvc.get(s, 0) < c:
                myvc[s] = c
        c_done = self.cnt.get(s_done, 0) + delta
        self.cnt[s_done] = c_done
        q.append(("i", fn, s_done, delta))
        sn = dict(myvc)
        sn[s_done] = c_done
        self.snap[(s_done, c_done)] = sn
        ev = (s_done, c_done)
        for w in writes:
            if acc:
                d = self.lastw.setdefault(w, {})
                if d.get(s_done, 0) < c_done:
                    d[s_done] = c_done
            else:
                self.lastw[w] = {s_done: c_done}
                self.readers[w] = {}
        for r in reads:
            d = self.readers.setdefault(r, {})
            if d.get(s_done, 0) < c_done:
                d[s_done] = c_done
        return ev

    def final_waits(self):
        for eng in ("sp", "act", "pool"):
            for i in range(NDMASEM):
                s = f"d:{eng}{i}"
                c = self.cnt.get(s, 0)
                if c:
                    self.q[eng].append(("w", s, c))


def build_nc(nseq=2, ntile=8, nexp=8, dumps=()):
    nc = bass.Bass("TRN2", target_bir_lowering=False)
    NTOK = ntile * TT

    x_d = nc.dram_tensor("x", [nseq, NTOK, D], F32, kind="ExternalInput").ap()
    y_d = nc.dram_tensor("y", [nseq, NTOK, D], F32, kind="ExternalOutput").ap()
    w_in = {}
    for nm in ("wq", "wk", "wv", "wo"):
        w_in[nm] = nc.dram_tensor(nm, [KCN, 128, D], F32, kind="ExternalInput").ap()
    w_in["sw1"] = nc.dram_tensor("sw1", [NHC, 128, D], F32, kind="ExternalInput").ap()
    w_in["sw2"] = nc.dram_tensor("sw2", [NHC, 128, D], F32, kind="ExternalInput").ap()
    w_in["ew1"] = nc.dram_tensor("ew1", [nexp * NHC, 128, D], F32, kind="ExternalInput").ap()
    w_in["ew2"] = nc.dram_tensor("ew2", [nexp * NHC, 128, D], F32, kind="ExternalInput").ap()
    bo_d = nc.dram_tensor("bo", [1, D], F32, kind="ExternalInput").ap()
    projt_d = nc.dram_tensor("projt", [DH, M], F32, kind="ExternalInput").ap()
    gatet_d = nc.dram_tensor("gatet", [128, 64], F32, kind="ExternalInput").ap()

    w16 = {}
    for nm, src in w_in.items():
        w16[nm] = nc.dram_tensor(nm + "16", list(src.shape), BF16, kind="Internal").ap()
    xnt_d = nc.dram_tensor("xnt", [nseq, ntile, KCN, 128, TT], BF16, kind="Internal").ap()
    stabscr = nc.dram_tensor("stabscr", [1, H], F32, kind="Internal").ap()

    dump_d = {}
    for nm, shp in dumps:
        dump_d[nm] = nc.dram_tensor("dump_" + nm, shp, F32, kind="ExternalOutput").ap()

    em = Em()
    st = ExitStack()
    sb = lambda name, shape, dt: st.enter_context(nc.sbuf_tensor(name, shape, dt))
    with st:
        ident16 = sb("ident16", [128, 128], BF16)
        identf = sb("identf", [128, 128], F32)
        ones16 = sb("ones16", [1, 128], BF16)
        bo16 = sb("bo16", [1, D], BF16)
        projt16 = sb("projt16", [128, M], BF16)
        gatet16 = sb("gatet16", [128, 64], BF16)
        gates32 = sb("gates32", [128, 64], F32)
        msk = sb("msk", [128, 128], BF16)
        stab_b = sb("stab_b", [128, H], F32)
        rowmax = sb("rowmax", [128, H], F32)
        stabcol = sb("stabcol", [H, 1], F32)
        stabrow = sb("stabrow", [1, H], F32)
        ctxacc = sb("ctxacc", [128, H * 3 * 65], F32)
        ctx16 = sb("ctx16", [128, H * 3 * 65], BF16)
        xt = sb("xt", [128, G, D], F32)
        lnst = sb("lnst", [128, G, 8], F32)
        xnT = sb("xnT", [128, 2, KCN, TT], BF16)
        kqT = sb("kqT", [128, KCN, TT], BF16)
        k2b = sb("k2b", [128, TT], BF16)
        vext = sb("vext", [128, G, H * 65], BF16)
        kp16 = sb("kp16", [128, 16, M], BF16)
        qpT16 = sb("qpT16", [128, 4, 3 * 128], BF16)
        etmp = sb("etmp", [128, 2, M], F32)
        oext = sb("oext", [65, 4, 128], F32)
        nbc = sb("nbc", [128, 16], F32)
        diagT = sb("diagT", [128, G, H], F32)
        glog = sb("glog", [128, G, 8], F32)
        ge = sb("ge", [128, G, 8], F32)
        gsc = sb("gsc", [128, G, 16], F32)
        gq8 = sb("gq8", [128, G, 24], F32)
        attn_cat = sb("attn_cat", [128, G, D], BF16)
        xn16 = attn_cat
        acT = sb("acT", [128, 2, KCN, 128], BF16)
        outacc = sb("outacc", [128, G, D], F32)
        dump = outacc[:, 0]
        siluh = sb("siluh", [128, NHC, TT], BF16)
        w1r = sb("w1r", [128, 4, D], BF16)
        w2r = sb("w2r", [128, 4, D], BF16)
        wA = sb("wA", [128, KCN, D], BF16)
        wB = sb("wB", [128, KCN, D], BF16)
        cs32 = sb("cs32", [128, 4, 1, D // 2], F32)
        cs16 = sb("cs16", [128, 2, 1, D // 2], BF16)

        pd = [st.enter_context(nc.psum_tensor(f"pd{i}", [128, 1024], F32)) for i in range(4)]

        def pbank(j):
            return pd[j // 2][:, (j % 2) * 512:(j % 2) * 512 + 512]

        def pbank16(j):
            return pd[j // 2][:].bitcast(BF16)[:, (j % 2) * 1024:(j % 2) * 1024 + 1024]

        def pdfull(i):
            return pd[i][:]

        sem = {}
        for e in ENGINES:
            sem["e:" + e] = st.enter_context(nc.semaphore("s_" + e))
        for e in ("sp", "act", "pool"):
            for i in range(NDMASEM):
                sem[f"d:{e}{i}"] = st.enter_context(nc.semaphore(f"s_d_{e}{i}"))

        em.emit("pool", lambda g: g.memset(ident16[:], 0.0), writes=("ident16",))
        em.emit("pool", lambda g: g.affine_select(
            out=ident16[:], in_=ident16[:], compare_op=ALU.not_equal, fill=1.0,
            base=0, pattern=[[-1, 128]], channel_multiplier=1),
            reads=("ident16",), writes=("ident16",))
        em.emit("pool", lambda g: g.memset(identf[:], 0.0), writes=("identf",))
        em.emit("pool", lambda g: g.affine_select(
            out=identf[:], in_=identf[:], compare_op=ALU.not_equal, fill=1.0,
            base=0, pattern=[[-1, 128]], channel_multiplier=1),
            reads=("identf",), writes=("identf",))
        em.emit("dve", lambda v: v.memset(ones16[:], 1.0), writes=("ones16",))
        vext_ones = vext[:].rearrange("p g (h x) -> p g h x", x=65)[:, :, :, 64:65]
        em.emit("dve", lambda v: v.memset(vext_ones, 1.0), writes=("vext_ones",))
        em.emit("dve", lambda v: v.memset(msk[:], 0.0), writes=("msk",))
        for kc in range(KCN):
            c0 = kc * 16 + 2 * kc
            c1 = kc * 16 + 2 * kc + 1
            em.emit("dve", lambda v, c0=c0: v.memset(msk[0:64, c0:c0 + 1], 1.0 / 16.0),
                    reads=("msk",), writes=("msk",))
            em.emit("dve", lambda v, c1=c1: v.memset(msk[64:128, c1:c1 + 1], 1.0 / 16.0),
                    reads=("msk",), writes=("msk",))
        em.emit("sp", lambda s: s.dma_start(out=ctxacc[0:DH, 0:M], in_=projt_d),
                dma=True, writes=("projs",))
        em.emit("dve", lambda v: v.tensor_scalar_mul(projt16[0:DH, :], ctxacc[0:DH, 0:M], DN),
                reads=("projs",), writes=("projt16",))
        em.emit("dve", lambda v: v.tensor_scalar_mul(projt16[DH:128, :], ctxacc[0:DH, 0:M], DN),
                reads=("projs",), writes=("projt16",), acc=True)
        em.emit("dve", lambda v: v.memset(ctxacc[:], 0.0),
                writes=tuple(f"ctx_{h}_{c}" for h in range(H) for c in range(3)) + ("projs",))
        em.emit("sp", lambda s: s.dma_start(out=gates32[:], in_=gatet_d), dma=True, writes=("gates32",))
        em.emit("dve", lambda v: v.tensor_copy(out=gatet16[:], in_=gates32[:]),
                reads=("gates32",), writes=("gatet16",))
        em.emit("sp", lambda s: s.dma_start(out=dump[0:1, :], in_=bo_d), dma=True, writes=("oa0",))
        em.emit("dve", lambda v: v.tensor_copy(out=bo16[:], in_=dump[0:1, :]),
                reads=("oa0",), writes=("bo16",))

        conv_plan = []
        for nm in ("wk", "wv", "wq", "wo", "sw1", "sw2"):
            conv_plan.append((w_in[nm], w16[nm], "d:" + nm, w_in[nm].shape[0]))
        for e in range(nexp):
            conv_plan.append((w_in["ew1"][e * NHC:(e + 1) * NHC], w16["ew1"][e * NHC:(e + 1) * NHC],
                              f"d:ew1_{e}", NHC))
            conv_plan.append((w_in["ew2"][e * NHC:(e + 1) * NHC], w16["ew2"][e * NHC:(e + 1) * NHC],
                              f"d:ew2_{e}", NHC))
        chunks = []
        cend = {}
        for csrc, cdst, cregion, nb in conv_plan:
            for b0 in range(nb):
                for cd in (0, D // 2):
                    chunks.append((csrc[b0:b0 + 1, :, cd:cd + D // 2],
                                   cdst[b0:b0 + 1, :, cd:cd + D // 2], cregion))
            cend[cregion] = len(chunks)
        cst = {"load": 0, "done": 0}

        def emit_conv_load(i):
            csrc = chunks[i][0]
            ln = i % 4
            em.emit("pool", lambda q, csrc=csrc, ln=ln: q.dma_start(
                out=cs32[:, ln], in_=csrc.rearrange("b p f -> p b f")),
                dma=True, writes=(f"cs32{ln}",))

        def emit_conversions(n):
            while n > 0 and cst["done"] < len(chunks):
                while cst["load"] < len(chunks) and cst["load"] - cst["done"] < 4:
                    emit_conv_load(cst["load"])
                    cst["load"] += 1
                i = cst["done"]
                _, cdst, cregion = chunks[i]
                ln = i % 4
                par = i % 2
                if ln % 2 == 0:
                    em.emit("dve", lambda v, ln=ln, par=par: v.tensor_copy(
                        out=cs16[:, par], in_=cs32[:, ln]),
                        reads=(f"cs32{ln}",), writes=(f"cs16{par}",))
                else:
                    em.emit("act", lambda a, ln=ln, par=par: a.copy(
                        out=cs16[:, par], in_=cs32[:, ln]),
                        reads=(f"cs32{ln}",), writes=(f"cs16{par}",))
                em.emit("pool", lambda q, cdst=cdst, par=par: q.dma_start(
                    out=cdst.rearrange("b p f -> p b f"), in_=cs16[:, par]),
                    dma=True, reads=(f"cs16{par}",), writes=(cregion,), acc=True)
                cst["done"] += 1
                n -= 1

        def need_conv(region):
            end = cend.get(region)
            if end is not None and cst["done"] < end:
                emit_conversions(end - cst["done"])

        emit_conversions(16)

        def load_w(buf, buf_region, name16):
            need_conv("d:" + name16)
            em.emit("sp", lambda s: s.dma_start(
                out=(wA if buf == "A" else wB)[:],
                in_=w16[name16].rearrange("k p d -> p k d")),
                dma=True, reads=("d:" + name16,), writes=(buf_region,))

        def dump_sbuf(nm, ap, reads):
            if nm in dump_d:
                em.emit("sp", lambda s: s.dma_start(out=dump_d[nm], in_=ap),
                        dma=True, reads=reads)

        def emit_proj(wbuf, wreg, xslot, out_sb, out_region):
            wt = wA if wbuf == "A" else wB
            for cp in range(KCN // 2):
                pdi = cp % 2
                for half in range(2):
                    c = 2 * cp + half

                    def f(t, c=c, pdi=pdi, half=half, xslot=xslot, wt=wt):
                        ins = None
                        for kc in range(KCN):
                            ins = t.matmul(pbank(pdi * 2 + half),
                                           wt[:, kc, c * 128:(c + 1) * 128],
                                           xnT[:, xslot, kc, :], start=(kc == 0),
                                           stop=(kc == KCN - 1))
                        return ins
                    em.emit("pe", f, reads=(wreg, f"xnT{xslot}"),
                            writes=(f"pb{pdi * 2 + half}",))
                em.emit("act", lambda s, cp=cp, pdi=pdi: s.copy(
                    out=out_sb[:, 2 * cp:2 * cp + 2, :],
                    in_=pdfull(pdi).rearrange("p (b f) -> p b f", f=512)),
                    reads=(f"pb{pdi * 2}", f"pb{pdi * 2 + 1}"),
                    writes=(out_region + str(2 * cp), out_region + str(2 * cp + 1)))

        def emit_dash(h, g, bank, kq_region):
            pp = 64 * (h % 2)
            em.emit("pe", lambda t, h=h, g=g, bank=bank, pp=pp: t.matmul(
                pbank(bank)[:, 0:M],
                kqT[pp:pp + DH, h // 2, g * 128:(g + 1) * 128],
                projt16[pp:pp + DH, :], start=True, stop=True),
                reads=(kq_region, "projt16"), writes=(f"pb{bank}",))

        def pass_a(s):
            em.emit("dve", lambda v: v.memset(rowmax[:], -1e30),
                    reads=("rowmax",), writes=("rowmax",))
            load_w("A", "wA", "wk")
            for j in range(ntile):
                xsl = j % 2
                t0 = j * TT
                em.emit("sp", lambda sp, s=s, t0=t0: sp.dma_start(
                    out=xt[:], in_=x_d[s, t0:t0 + TT, :].rearrange("(g p) d -> p g d", p=128)),
                    dma=True, writes=("xt",))
                for g in range(G):
                    ls = lnst[:, g]
                    em.emit("act", lambda a, g=g, ls=ls: a.activation(
                        out=dump[:], in_=xt[:, g, :], func=AF.Identity, accum_out=ls[:, 0:1]),
                        reads=("xt",), writes=("oa0", f"ln{g}"))
                    em.emit("act", lambda a, g=g, ls=ls: a.activation(
                        out=dump[:], in_=xt[:, g, :], func=AF.Square, accum_out=ls[:, 1:2]),
                        reads=("xt",), writes=("oa0", f"ln{g}b"))
                    em.emit("dve", lambda v, ls=ls: v.tensor_scalar_mul(
                        ls[:, 2:3], ls[:, 0:1], -1.0 / D),
                        reads=(f"ln{g}",), writes=(f"ln{g}",))
                    em.emit("dve", lambda v, ls=ls: v.tensor_scalar(
                        ls[:, 3:4], ls[:, 1:2], 1.0 / D, LN_EPS, op0=ALU.mult, op1=ALU.add),
                        reads=(f"ln{g}b",), writes=(f"ln{g}b",))
                    em.emit("dve", lambda v, ls=ls: v.tensor_tensor(
                        ls[:, 5:6], ls[:, 2:3], ls[:, 2:3], ALU.mult),
                        reads=(f"ln{g}",), writes=(f"ln{g}c",))
                    em.emit("dve", lambda v, ls=ls: v.tensor_tensor(
                        ls[:, 3:4], ls[:, 3:4], ls[:, 5:6], ALU.subtract),
                        reads=(f"ln{g}b", f"ln{g}c"), writes=(f"ln{g}b",))
                    em.emit("dve", lambda v, ls=ls: v.reciprocal(ls[:, 6:7], ls[:, 3:4]),
                        reads=(f"ln{g}b",), writes=(f"ln{g}d",))
                    em.emit("act", lambda a, ls=ls: a.activation(
                        out=ls[:, 4:5], in_=ls[:, 6:7], func=AF.Sqrt),
                        reads=(f"ln{g}d",), writes=(f"ln{g}e",))
                    em.emit("dve", lambda v, g=g, ls=ls: v.tensor_scalar(
                        xn16[:, g, :], xt[:, g, :], ls[:, 2:3], ls[:, 4:5],
                        op0=ALU.add, op1=ALU.mult),
                        reads=("xt", f"ln{g}", f"ln{g}e"), writes=(f"xnac{g}",))
                for g in range(G):
                    for c in range(KCN):
                        bank = 4 + ((g * KCN + c) % 4)
                        em.emit("pe", lambda t, g=g, c=c, bank=bank: t.transpose(
                            pbank16(bank)[:, 0:128], xn16[:, g, c * 128:(c + 1) * 128], ident16[:]),
                            reads=(f"xnac{g}", "ident16"), writes=(f"pb{bank}",))
                        em.emit("dve", lambda v, g=g, c=c, bank=bank, xsl=xsl: v.tensor_copy(
                            out=xnT[:, xsl, c, g * 128:(g + 1) * 128], in_=pbank16(bank)[:, 0:128]),
                            reads=(f"pb{bank}",), writes=(f"xnT{xsl}",), acc=(g + c > 0))
                em.emit("sp", lambda sp, s=s, j=j, xsl=xsl: sp.dma_start(
                    out=xnt_d[s, j].rearrange("c p t -> p c t"), in_=xnT[:, xsl]),
                    dma=True, reads=(f"xnT{xsl}",), writes=(f"d:xnt{s}_{j}",))
                emit_proj("A", "wA", xsl, kqT, "kqT")
                for h in range(H):
                    for g in range(G):
                        bank = (h * G + g) % 4
                        emit_dash(h, g, bank, f"kqT{h // 2}")
                        em.emit("dve", lambda v, g=g, bank=bank: v.reduce_max(
                            nbc[:, 8 + g:9 + g], pbank(bank)[:, 0:M], axis=AX.X),
                            reads=(f"pb{bank}",), writes=(f"rm{g}",))
                    em.emit("dve", lambda v: v.tensor_tensor(
                        nbc[:, 12:13], nbc[:, 8:9], nbc[:, 9:10], ALU.max),
                        reads=("rm0", "rm1"), writes=("rm4",))
                    em.emit("dve", lambda v: v.tensor_tensor(
                        nbc[:, 13:14], nbc[:, 10:11], nbc[:, 11:12], ALU.max),
                        reads=("rm2", "rm3"), writes=("rm5",))
                    em.emit("dve", lambda v: v.tensor_tensor(
                        nbc[:, 12:13], nbc[:, 12:13], nbc[:, 13:14], ALU.max),
                        reads=("rm4", "rm5"), writes=("rm4",))
                    em.emit("dve", lambda v, h=h: v.tensor_tensor(
                        rowmax[:, h:h + 1], rowmax[:, h:h + 1], nbc[:, 12:13], ALU.max),
                        reads=("rowmax", "rm4"), writes=("rowmax",))
                emit_conversions(64)
            em.emit("pe", lambda t: t.transpose(pbank(0)[0:H, 0:128], rowmax[:], identf[:]),
                    reads=("rowmax", "identf"), writes=("pb0",))
            em.emit("dve", lambda v: v.reduce_max(stabcol[:], pbank(0)[0:H, 0:128], axis=AX.X),
                    reads=("pb0",), writes=("stabcol",))
            em.emit("pe", lambda t: t.transpose(pbank(1)[0:1, 0:H], stabcol[:], identf[0:16, 0:16]),
                    reads=("stabcol", "identf"), writes=("pb1",))
            em.emit("dve", lambda v: v.tensor_copy(out=stabrow[:], in_=pbank(1)[0:1, 0:H]),
                    reads=("pb1",), writes=("stabrow",))
            em.emit("sp", lambda sp: sp.dma_start(out=stabscr, in_=stabrow[:]),
                    dma=True, reads=("stabrow",), writes=("d:stabscr",))
            bcast_ap = bass.AP(stabscr.tensor, stabscr.offset, [[0, 128], [1, H]])
            em.emit("sp", lambda sp: sp.dma_start(out=stab_b[:], in_=bcast_ap),
                    dma=True, reads=("d:stabscr",), writes=("stab_b",))
            dump_sbuf("stab", stab_b[0:1, :], ("stab_b",))

        def pass_b(s):
            load_w("A", "wA", "wk")
            load_w("B", "wB", "wv")
            em.emit("sp", lambda sp, s=s: sp.dma_start(
                out=xnT[:, 0], in_=xnt_d[s, 0].rearrange("c p t -> p c t")),
                dma=True, reads=(f"d:xnt{s}_0",), writes=("xnT0",))
            for j in range(ntile):
                xsl = j % 2
                emit_proj("A", "wA", xsl, kqT, "kqT")
                if j + 1 < ntile:
                    em.emit("sp", lambda sp, s=s, j=j: sp.dma_start(
                        out=xnT[:, (j + 1) % 2],
                        in_=xnt_d[s, j + 1].rearrange("c p t -> p c t")),
                        dma=True, reads=(f"d:xnt{s}_{j + 1}",),
                        writes=(f"xnT{(j + 1) % 2}",))
                for g in range(G):
                    pdi = g % 2
                    for nh in range(2):
                        def f(t, g=g, nh=nh, pdi=pdi, xsl=xsl):
                            ins = None
                            for kc in range(KCN):
                                ins = t.matmul(pbank(pdi * 2 + nh),
                                               xnT[:, xsl, kc, g * 128:(g + 1) * 128],
                                               wB[:, kc, nh * 512:(nh + 1) * 512],
                                               start=(kc == 0), stop=(kc == KCN - 1))
                            return ins
                        em.emit("pe", f, reads=(f"xnT{xsl}", "wB"),
                                writes=(f"pb{pdi * 2 + nh}",))
                    dst = vext[:, g, :].rearrange("p (h x) -> p h x", x=65)[:, :, 0:64]
                    em.emit("dve", lambda v, dst=dst, pdi=pdi: v.tensor_copy(
                        out=dst, in_=pdfull(pdi).rearrange("p (h x) -> p h x", x=64)),
                        reads=(f"pb{pdi * 2}", f"pb{pdi * 2 + 1}"), writes=(f"vext{g}",))
                for kc in range(KCN):
                    em.emit("act", lambda a, kc=kc: a.activation(
                        out=k2b[:], in_=kqT[:, kc, :], func=AF.Square),
                        reads=(f"kqT{kc}",), writes=("k2b",))
                    for g in range(G):
                        em.emit("pe", lambda t, kc=kc, g=g: t.matmul(
                            pbank(4 + g)[:, 0:16], k2b[:, g * 128:(g + 1) * 128],
                            msk[:, kc * 16:kc * 16 + 16],
                            start=(kc == 0), stop=(kc == KCN - 1)),
                            reads=("k2b", "msk"), writes=(f"pb{4 + g}",))
                for g in range(G):
                    em.emit("dve", lambda v, g=g: v.tensor_copy(
                        out=diagT[:, g, :], in_=pbank(4 + g)[:, 0:16]),
                        reads=(f"pb{4 + g}",), writes=(f"diag{g}",))

                def ph1(ph):
                    for u in range(8):
                        h = 2 * ph + u // 4
                        g = u % 4
                        slot = (ph % 2) * 8 + u
                        bank = slot % 4
                        par = slot % 2
                        emit_dash(h, g, bank, f"kqT{h // 2}")
                        em.emit("dve", lambda v, h=h, g=g, par=par: v.scalar_tensor_tensor(
                            out=nbc[:, par:par + 1], in0=diagT[:, g, h:h + 1], scalar=-1.0,
                            in1=stab_b[:, h:h + 1], op0=ALU.mult, op1=ALU.subtract),
                            reads=(f"diag{g}", "stab_b"), writes=(f"nb{par}",))
                        em.emit("act", lambda a, bank=bank, par=par: a.activation(
                            out=etmp[:, par, :], in_=pbank(bank)[:, 0:M], func=AF.Exp,
                            bias=nbc[:, par:par + 1], scale=1.0),
                            reads=(f"pb{bank}", f"nb{par}"), writes=(f"etmp{par}",))
                        em.emit("dve", lambda v, slot=slot, par=par: v.tensor_scalar(
                            kp16[:, slot, :], etmp[:, par, :], EPS, None, op0=ALU.add),
                            reads=(f"etmp{par}",), writes=(f"kp{slot}",))

                def ph2(ph, j=j):
                    for u2 in range(2):
                        h = 2 * ph + u2
                        s0 = (ph % 2) * 8 + u2 * 4
                        for c in range(3):
                            mc = MCS[c]
                            bank = 6 + (c % 2)

                            def f(t, h=h, c=c, mc=mc, bank=bank, s0=s0):
                                ins = None
                                for g in range(G):
                                    ins = t.matmul(pbank(bank)[0:mc, 0:65],
                                                   kp16[:, s0 + g, c * 128:c * 128 + mc],
                                                   vext[:, g, h * 65:h * 65 + 65],
                                                   start=(g == 0), stop=(g == G - 1))
                                return ins
                            em.emit("pe", f,
                                    reads=tuple(f"kp{s0 + g}" for g in range(G)) +
                                          ("vext0", "vext1", "vext2", "vext3"),
                                    writes=(f"pb{bank}",))
                            co = (h * 3 + c) * 65
                            if j == 0:
                                em.emit("dve", lambda v, mc=mc, bank=bank, co=co: v.tensor_copy(
                                    out=ctxacc[0:mc, co:co + 65], in_=pbank(bank)[0:mc, 0:65]),
                                    reads=(f"pb{bank}",), writes=(f"ctx_{h}_{c}",))
                            else:
                                em.emit("dve", lambda v, mc=mc, bank=bank, co=co: v.tensor_tensor(
                                    ctxacc[0:mc, co:co + 65], ctxacc[0:mc, co:co + 65],
                                    pbank(bank)[0:mc, 0:65], ALU.add),
                                    reads=(f"pb{bank}", f"ctx_{h}_{c}"), writes=(f"ctx_{h}_{c}",))

                for ph in range(H // 2):
                    ph1(ph)
                    if ph >= 1:
                        ph2(ph - 1)
                ph2(H // 2 - 1)
                emit_conversions(64)
            em.emit("dve", lambda v: v.tensor_copy(out=ctx16[:], in_=ctxacc[:]),
                    reads=tuple(f"ctx_{h}_{c}" for h in range(H) for c in range(3)),
                    writes=("ctx16",))
            dump_sbuf("ctx", ctxacc[:], ("ctx16",))

        NU = G * H

        def c_fg_tile(s, j, xsl, t0):
            em.emit("sp", lambda sp, s=s, t0=t0: sp.dma_start(
                out=xt[:], in_=x_d[s, t0:t0 + TT, :].rearrange("(g p) d -> p g d", p=128)),
                dma=True, writes=("xt",))
            for cp in range(KCN // 2):
                for half in range(2):
                    c = 2 * cp + half

                    def fpr(t, c=c, half=half, xsl=xsl):
                        ins = None
                        for kc in range(KCN):
                            ins = t.matmul(pbank(2 + half), wA[:, kc, c * 128:(c + 1) * 128],
                                           xnT[:, xsl, kc, :], start=(kc == 0),
                                           stop=(kc == KCN - 1))
                        return ins
                    em.emit("pe", fpr, reads=("wA", f"xnT{xsl}"), writes=(f"pb{2 + half}",))
                em.emit("act", lambda a, cp=cp: a.copy(
                    out=kqT[:, 2 * cp:2 * cp + 2, :],
                    in_=pdfull(1).rearrange("p (b f) -> p b f", f=512)),
                    reads=("pb2", "pb3"),
                    writes=(f"kqT{2 * cp}", f"kqT{2 * cp + 1}"))
                yield
            for kc in range(KCN):
                em.emit("act", lambda a, kc=kc: a.activation(
                    out=k2b[:], in_=kqT[:, kc, :], func=AF.Square),
                    reads=(f"kqT{kc}",), writes=("k2b",))
                for g in range(G):
                    em.emit("pe", lambda t, kc=kc, g=g: t.matmul(
                        pbank(4)[:, g * 16:g * 16 + 16], k2b[:, g * 128:(g + 1) * 128],
                        msk[:, kc * 16:kc * 16 + 16],
                        start=(kc == 0), stop=(kc == KCN - 1)),
                        reads=("k2b", "msk"), writes=("pb4",), acc=True)
                if kc % 2 == 1:
                    yield
            em.emit("dve", lambda v: v.tensor_copy(
                out=diagT[:], in_=pbank(4)[:, 0:64].rearrange("p (g h) -> p g h", h=16)),
                reads=("pb4",), writes=tuple(f"diag{g}" for g in range(G)))
            yield
            for g in range(G):
                def fga(t, g=g, xsl=xsl):
                    ins = None
                    for kc in range(KCN):
                        ins = t.matmul(pbank(5)[:, g * 8:g * 8 + 8],
                                       xnT[:, xsl, kc, g * 128:(g + 1) * 128],
                                       gatet16[:, kc * 8:kc * 8 + 8],
                                       start=(kc == 0), stop=(kc == KCN - 1))
                    return ins
                em.emit("pe", fga, reads=(f"xnT{xsl}", "gatet16"), writes=("pb5",), acc=True)
            em.emit("dve", lambda v: v.tensor_copy(
                out=glog[:], in_=pbank(5)[:, 0:32].rearrange("p (g e) -> p g e", e=8)),
                reads=("pb5",), writes=tuple(f"glog{g}" for g in range(G)))
            yield
            for g in range(G):
                gg = glog[:, g]
                sc = gsc[:, g]
                q8 = gq8[:, g]
                R = (f"glog{g}",)
                em.emit("dve", lambda v, gg=gg, sc=sc: v.reduce_max(sc[:, 0:1], gg, axis=AX.X),
                        reads=R, writes=(f"gm{g}",))
                em.emit("dve", lambda v, gg=gg, sc=sc, q8=q8: v.tensor_scalar(
                    q8[:, 0:8], gg, sc[:, 0:1], None, op0=ALU.is_equal),
                    reads=R + (f"gm{g}",), writes=(f"geq1{g}",))
                em.emit("dve", lambda v, gg=gg, q8=q8: v.scalar_tensor_tensor(
                    out=q8[:, 16:24], in0=q8[:, 0:8], scalar=-1e30, in1=gg,
                    op0=ALU.mult, op1=ALU.add),
                    reads=R + (f"geq1{g}",), writes=(f"gmsk{g}",))
                em.emit("dve", lambda v, sc=sc, q8=q8: v.reduce_max(
                    sc[:, 1:2], q8[:, 16:24], axis=AX.X),
                    reads=(f"gmsk{g}",), writes=(f"gm2{g}",))
                em.emit("dve", lambda v, gg=gg, sc=sc, q8=q8: v.tensor_scalar(
                    q8[:, 8:16], gg, sc[:, 1:2], None, op0=ALU.is_equal),
                    reads=R + (f"gm2{g}",), writes=(f"geq2{g}",))
                em.emit("dve", lambda v, sc=sc: v.tensor_tensor(
                    sc[:, 2:3], sc[:, 1:2], sc[:, 0:1], ALU.subtract),
                    reads=(f"gm{g}", f"gm2{g}"), writes=(f"gdm{g}",))
                em.emit("act", lambda a, sc=sc: a.activation(
                    out=sc[:, 3:4], in_=sc[:, 2:3], func=AF.Exp),
                    reads=(f"gdm{g}",), writes=(f"ge12{g}",))
                em.emit("dve", lambda v, sc=sc: v.tensor_scalar(
                    sc[:, 4:5], sc[:, 3:4], 1.0, None, op0=ALU.add),
                    reads=(f"ge12{g}",), writes=(f"gden{g}",))
                em.emit("dve", lambda v, sc=sc: v.reciprocal(sc[:, 5:6], sc[:, 4:5]),
                        reads=(f"gden{g}",), writes=(f"gg1{g}",))
                em.emit("dve", lambda v, sc=sc: v.tensor_scalar(
                    sc[:, 6:7], sc[:, 5:6], -1.0, 1.0, op0=ALU.mult, op1=ALU.add),
                    reads=(f"gg1{g}",), writes=(f"gg2{g}",))
                em.emit("dve", lambda v, sc=sc, q8=q8: v.tensor_scalar(
                    q8[:, 0:8], q8[:, 0:8], sc[:, 5:6], None, op0=ALU.mult),
                    reads=(f"geq1{g}", f"gg1{g}"), writes=(f"geq1{g}",))
                em.emit("dve", lambda v, g=g, sc=sc, q8=q8: v.scalar_tensor_tensor(
                    out=ge[:, g, :], in0=q8[:, 8:16], scalar=sc[:, 6:7], in1=q8[:, 0:8],
                    op0=ALU.mult, op1=ALU.add),
                    reads=(f"geq2{g}", f"gg2{g}", f"geq1{g}"), writes=(f"ge{g}",))
                yield
            units = [(g, h) for g in range(G) for h in range(H)]

            def st_dash(i):
                g, h = units[i]
                slot = i % 16
                bank = 4 + (i % 2)
                par = i % 2
                emit_dash(h, g, bank, f"kqT{h // 2}")
                em.emit("dve", lambda v, bank=bank, par=par: v.reduce_max(
                    nbc[:, 2 + par:3 + par], pbank(bank)[:, 0:M], axis=AX.X, negate=True),
                    reads=(f"pb{bank}",), writes=(f"nstab{par}",))
                em.emit("dve", lambda v, g=g, h=h, par=par: v.scalar_tensor_tensor(
                    out=nbc[:, par:par + 1], in0=diagT[:, g, h:h + 1], scalar=-1.0,
                    in1=nbc[:, 2 + par:3 + par], op0=ALU.mult, op1=ALU.add),
                    reads=(f"diag{g}", f"nstab{par}"), writes=(f"nb{par}",))
                em.emit("act", lambda a, bank=bank, par=par: a.activation(
                    out=etmp[:, par, :], in_=pbank(bank)[:, 0:M], func=AF.Exp,
                    bias=nbc[:, par:par + 1], scale=1.0),
                    reads=(f"pb{bank}", f"nb{par}"), writes=(f"etmp{par}",))
                em.emit("dve", lambda v, slot=slot, par=par: v.tensor_scalar(
                    kp16[:, slot, :], etmp[:, par, :], EPS, None, op0=ALU.add),
                    reads=(f"etmp{par}",), writes=(f"kp{slot}",))

            UB = (2, 3, 6, 7)

            def st_qpt(i):
                slot = i % 16
                p4 = i % 4
                b = UB[p4]

                def ftr(t, slot=slot, b=b):
                    ins = None
                    for c in range(3):
                        mc = MCS[c]
                        ins = t.transpose(pbank16(b)[0:mc, c * 128:c * 128 + 128],
                                          kp16[:, slot, c * 128:c * 128 + mc], ident16[:])
                    return ins
                em.emit("pe", ftr, reads=(f"kp{slot}", "ident16"), writes=(f"pb{b}",))
                em.emit("dve", lambda v, p4=p4, b=b: v.tensor_copy(
                    out=qpT16[:, p4, 0:256], in_=pbank16(b)[:, 0:256]),
                    reads=(f"pb{b}",), writes=(f"qpT{p4}",))
                em.emit("dve", lambda v, p4=p4, b=b: v.tensor_copy(
                    out=qpT16[0:10, p4, 256:384], in_=pbank16(b)[0:10, 256:384]),
                    reads=(f"pb{b}",), writes=(f"qpT{p4}",), acc=True)

            def st_fext(i):
                g, h = units[i]
                p4 = i % 4
                b = UB[p4]

                def fex(t, h=h, p4=p4, b=b):
                    ins = None
                    for c in range(3):
                        mc = MCS[c]
                        ins = t.matmul(pbank(b)[0:65, 192:320],
                                       ctx16[0:mc, (h * 3 + c) * 65:(h * 3 + c) * 65 + 65],
                                       qpT16[0:mc, p4, c * 128:c * 128 + 128],
                                       start=(c == 0), stop=(c == 2))
                    return ins
                em.emit("pe", fex, reads=("ctx16", f"qpT{p4}"), writes=(f"pb{b}",), acc=True)
                em.emit("act", lambda a, p4=p4, b=b: a.copy(
                    out=oext[:, p4, :], in_=pbank(b)[0:65, 192:320]),
                    reads=(f"pb{b}",), writes=(f"oext{p4}",))

            def st_otr(i):
                g, h = units[i]
                p4 = i % 4
                b = UB[p4]
                em.emit("pe", lambda t, p4=p4, b=b: t.transpose(
                    pbank(b)[:, 320:385], oext[:, p4, :], identf[0:65, 0:65]),
                    reads=(f"oext{p4}", "identf"), writes=(f"pb{b}",), acc=True)
                em.emit("dve", lambda v, p4=p4, b=b: v.reciprocal(
                    nbc[:, 4 + p4:5 + p4], pbank(b)[:, 384:385]),
                    reads=(f"pb{b}",), writes=(f"dinv{p4}",))
                em.emit("dve", lambda v, g=g, h=h, p4=p4, b=b: v.tensor_scalar(
                    attn_cat[:, g, h * 64:h * 64 + 64], pbank(b)[:, 320:384],
                    nbc[:, 4 + p4:5 + p4], None, op0=ALU.mult),
                    reads=(f"pb{b}", f"dinv{p4}"), writes=(f"xnac{g}",),
                    acc=(i >= G))

            for t in range(NU + 9):
                if 0 <= t - 8 < NU:
                    st_otr(t - 8)
                if 0 <= t - 6 < NU:
                    st_fext(t - 6)
                if 0 <= t - 4 < NU:
                    st_qpt(t - 4)
                if t < NU:
                    st_dash(t)
                if t == 32 and j + 1 < ntile:
                    em.emit("sp", lambda sp, s=s, j=j: sp.dma_start(
                        out=xnT[:, (j + 1) % 2],
                        in_=xnt_d[s, j + 1].rearrange("c p t -> p c t")),
                        dma=True, reads=(f"d:xnt{s}_{j + 1}",),
                        writes=(f"xnT{(j + 1) % 2}",))
                yield
            for g in range(G):
                wob = 4 + (g % 2)

                def ftr(t, g=g, wob=wob):
                    ins = None
                    for c in range(KCN):
                        ins = t.transpose(pbank16(wob)[:, c * 128:c * 128 + 128],
                                          attn_cat[:, g, c * 128:c * 128 + 128], ident16[:])
                    return ins
                em.emit("pe", ftr, reads=(f"xnac{g}", "ident16"), writes=(f"pb{wob}",))
                em.emit("dve", lambda v, g=g, wob=wob: v.tensor_copy(
                    out=acT[:, g % 2],
                    in_=pbank16(wob).rearrange("p (c f) -> p c f", f=128)),
                    reads=(f"pb{wob}",), writes=(f"acT{g % 2}",))
                yield
                for nh in range(2):
                    def fwo(t, g=g, nh=nh):
                        ins = None
                        for c in range(KCN):
                            ins = t.matmul(pbank(6 + nh), acT[:, g % 2, c, :],
                                           wB[:, c, nh * 512:(nh + 1) * 512],
                                           start=(c == 0), stop=False)
                        ins = t.matmul(pbank(6 + nh), ones16[:, 0:128],
                                       bo16[:, nh * 512:(nh + 1) * 512],
                                       start=False, stop=True)
                        return ins
                    em.emit("pe", fwo, reads=(f"acT{g % 2}", "wB", "ones16", "bo16"),
                            writes=(f"pb{6 + nh}",))
                em.emit("dve", lambda v, g=g: v.tensor_tensor(
                    outacc[:, g, :], outacc[:, g, :], pdfull(3), ALU.add),
                    reads=("pb6", "pb7", f"oa{g}"), writes=(f"oa{g}",))
                em.emit("dve", lambda v, g=g: v.tensor_tensor(
                    outacc[:, g, :], outacc[:, g, :], xt[:, g, :], ALU.add),
                    reads=("xt", f"oa{g}"), writes=(f"oa{g}",))
                yield

        def c_filler_tile(j, xsl):
            for e in range(-1, nexp):
                w1src = w16["sw1"] if e < 0 else w16["ew1"][e * NHC:(e + 1) * NHC]
                w2src = w16["sw2"] if e < 0 else w16["ew2"][e * NHC:(e + 1) * NHC]
                w1reg = "d:sw1" if e < 0 else f"d:ew1_{e}"
                w2reg = "d:sw2" if e < 0 else f"d:ew2_{e}"

                def load_w1(gi, w1src=w1src, w1reg=w1reg):
                    need_conv(w1reg)
                    sl = gi % 2
                    em.emit("sp", lambda sp, w1src=w1src, gi=gi, sl=sl: sp.dma_start(
                        out=w1r[:, 2 * sl:2 * sl + 2, :],
                        in_=w1src[2 * gi:2 * gi + 2].rearrange("b p f -> p b f")),
                        dma=True, reads=(w1reg,), writes=(f"w1r{sl}",))

                def load_w2(gi, w2src=w2src, w2reg=w2reg):
                    need_conv(w2reg)
                    sl = gi % 2
                    em.emit("act", lambda a, w2src=w2src, gi=gi, sl=sl: a.dma_start(
                        out=w2r[:, 2 * sl:2 * sl + 2, :],
                        in_=w2src[2 * gi:2 * gi + 2].rearrange("b p f -> p b f")),
                        dma=True, reads=(w2reg,), writes=(f"w2r{sl}",))

                load_w1(0)
                load_w1(1)
                for hc in range(NHC):
                    sl = (hc // 2) % 2
                    bank = hc % 2

                    def f1(t, hc=hc, sl=sl, bank=bank, xsl=xsl):
                        ins = None
                        for kc in range(KCN):
                            ins = t.matmul(pbank(bank),
                                           w1r[:, 2 * sl + hc % 2, kc * 128:kc * 128 + 128],
                                           xnT[:, xsl, kc, :],
                                           start=(kc == 0), stop=(kc == KCN - 1))
                        return ins
                    em.emit("pe", f1, reads=(f"w1r{sl}", f"xnT{xsl}"), writes=(f"pb{bank}",))
                    em.emit("act", lambda a, hc=hc, bank=bank: a.activation(
                        out=siluh[:, hc, :], in_=pbank(bank), func=AF.Silu),
                        reads=(f"pb{bank}",), writes=(f"siluh{hc}",))
                    if hc % 2 == 1:
                        hp = hc // 2
                        if hp + 2 < NHC // 2:
                            load_w1(hp + 2)
                        elif hp + 2 == NHC // 2:
                            load_w2(0)
                            load_w2(1)
                    yield True
                for hp in range(NHC // 2):
                    sl = hp % 2

                    def f2(t, hp=hp, sl=sl):
                        ins = None
                        for half in range(2):
                            hc = 2 * hp + half
                            for g in range(G):
                                for nh in range(2):
                                    ins = t.matmul(pbank(g * 2 + nh),
                                                   siluh[:, hc, g * 128:(g + 1) * 128],
                                                   w2r[:, 2 * sl + half, nh * 512:(nh + 1) * 512],
                                                   start=(hp == 0 and half == 0),
                                                   stop=(hp == NHC // 2 - 1 and half == 1))
                        return ins
                    em.emit("pe", f2,
                            reads=(f"siluh{2 * hp}", f"siluh{2 * hp + 1}", f"w2r{sl}"),
                            writes=tuple(f"pb{b}" for b in range(8)))
                    if hp + 2 < NHC // 2:
                        load_w2(hp + 2)
                    yield False
                for g in range(G):
                    if e < 0:
                        em.emit("dve", lambda v, g=g: v.tensor_copy(
                            out=outacc[:, g, :], in_=pdfull(g)),
                            reads=(f"pb{2 * g}", f"pb{2 * g + 1}"), writes=(f"oa{g}",))
                    else:
                        em.emit("dve", lambda v, g=g, e=e: v.scalar_tensor_tensor(
                            out=outacc[:, g, :], in0=pdfull(g), scalar=ge[:, g, e:e + 1],
                            in1=outacc[:, g, :], op0=ALU.mult, op1=ALU.add),
                            reads=(f"pb{2 * g}", f"pb{2 * g + 1}", f"ge{g}", f"oa{g}"),
                            writes=(f"oa{g}",))
                yield True

        def pass_c(s):
            emit_conversions(10 ** 9)
            load_w("A", "wA", "wq")
            load_w("B", "wB", "wo")
            em.emit("sp", lambda sp, s=s: sp.dma_start(
                out=xnT[:, 0], in_=xnt_d[s, 0].rearrange("c p t -> p c t")),
                dma=True, reads=(f"d:xnt{s}_0",), writes=("xnT0",))
            for j in range(ntile):
                xsl = j % 2
                t0 = j * TT
                fg = c_fg_tile(s, j, xsl, t0)
                fg_done = False
                ks = 0
                for safe in c_filler_tile(j, xsl):
                    if safe:
                        if ks % 3 != 2 and not fg_done:
                            try:
                                next(fg)
                            except StopIteration:
                                fg_done = True
                        ks += 1
                while not fg_done:
                    try:
                        next(fg)
                    except StopIteration:
                        fg_done = True
                em.emit("sp", lambda sp, s=s, t0=t0: sp.dma_start(
                    out=y_d[s, t0:t0 + TT, :].rearrange("(g p) d -> p g d", p=128),
                    in_=outacc[:]),
                    dma=True, reads=tuple(f"oa{g}" for g in range(G)), writes=())

        for s in range(nseq):
            pass_a(s)
            pass_b(s)
            pass_c(s)
        assert cst["done"] == len(chunks), "unconverted weights"
        em.final_waits()

        engmap = {"pe": "tensor", "dve": "vector", "act": "scalar",
                  "pool": "gpsimd", "sp": "sync"}
        with nc.Block() as blk:
            for e in ENGINES:
                ops = em.q[e]
                if not ops:
                    continue

                def runner(obj, ops=ops):
                    for op in ops:
                        if op[0] == "w":
                            obj.wait_ge(sem[op[1]], op[2])
                        else:
                            ins = op[1](obj)
                            ins.then_inc(sem[op[2]], op[3])
                getattr(blk, engmap[e])(runner)
    return nc


def _host_weights(inputs, nexp=8):
    c = np.ascontiguousarray
    w = {}
    w["wq"] = inputs["L0_attn_Wq"].reshape(KCN, 128, D)
    w["wk"] = inputs["L0_attn_Wk"].reshape(KCN, 128, D)
    w["wv"] = inputs["L0_attn_Wv"].reshape(KCN, 128, D)
    w["wo"] = inputs["L0_attn_Wo"].reshape(KCN, 128, D)
    w["bo"] = inputs["L0_attn_bo"].reshape(1, D)
    w["projt"] = c(inputs["L0_attn_proj"].T)
    gt = c(inputs["L0_ffn_gate_w"].T).reshape(KCN, 128, 8).transpose(1, 0, 2).reshape(128, 64)
    w["gatet"] = c(gt)
    w["sw1"] = c(inputs["L0_ffn_shared_w1"].reshape(NHC, 128, KCN, 128).transpose(0, 3, 2, 1)
                 ).reshape(NHC, 128, D)
    w["sw2"] = c(inputs["L0_ffn_shared_w2"].T).reshape(NHC, 128, D)
    ew1 = inputs["L0_ffn_experts_w1"][:nexp]
    w["ew1"] = c(ew1.reshape(nexp, NHC, 128, KCN, 128).transpose(0, 1, 4, 3, 2)
                 ).reshape(nexp * NHC, 128, D)
    ew2 = inputs["L0_ffn_experts_w2"][:nexp]
    w["ew2"] = c(ew2.transpose(0, 2, 1)).reshape(nexp * NHC, 128, D)
    w = {k: np.ascontiguousarray(v, dtype=np.float32) for k, v in w.items()}
    return w


def kernel(**inputs):
    x = np.ascontiguousarray(inputs["L0_attn_x"], dtype=np.float32)
    B = x.shape[0]
    ncores = 8
    per = B // ncores
    w = _host_weights(inputs)
    nc = build_nc(nseq=per, ntile=N // TT, nexp=8)
    in_maps = []
    for ci in range(ncores):
        m = {"x": x[ci * per:(ci + 1) * per]}
        m.update(w)
        in_maps.append(m)
    res = run_bass_kernel_spmd(nc, in_maps, core_ids=list(range(ncores)))
    return np.concatenate([r["y"] for r in res.results], axis=0)


if __name__ == "__main__":
    import reference as _ref
    ins = _ref.setup_inputs(0)
    ins = {k: np.asarray(v) for k, v in ins.items()}
    out = kernel(**ins)
    exp = np.asarray(_ref.reference(**ins))
    err = np.sqrt(((out - exp) ** 2).mean() / (exp ** 2).mean())
    print("resvar-rel:", err)
